# version 32
# baseline (speedup 1.0000x reference)
import numpy as np
from contextlib import ExitStack

import concourse.bass as bass
import concourse.mybir as mybir
from concourse.bass_utils import run_bass_kernel_spmd

F32 = mybir.dt.float32
BF16 = mybir.dt.bfloat16
I32 = mybir.dt.int32
I8 = mybir.dt.int8
AF = mybir.ActivationFunctionType
ALU = mybir.AluOpType

NCORES = 8
DM = 1024
DC = 8
HID = 3072
HC = 24
NH = 16
HD = 64
WIN = 256
T = 2048
ST = 1024
NST = T // ST
NBLK = ST // WIN
NPAIR = 8
CSHIFT = 32.0
QSCALE = 0.125
FS = 1.0 / 127.0
EPS = 1e-5
RN = 1.0 / DM

ENG_SEMS = ("t", "v", "c", "g")


class Prog:

    def __init__(self):
        self.ops = []
        self.wmap = {}
        self.rmap = {}
        self.cnt = {}
        self.credit = {}
        self.sem_names = set(ENG_SEMS)

    def _deps(self, body, reads, writes):
        need, need_tok = {}, {}

        def take(tok):
            if tok is None:
                return
            sem, val, _clock = tok
            if need.get(sem, 0) < val:
                need[sem] = val
                need_tok[sem] = tok

        for s in reads:
            take(self.wmap.get(s))
        for s in writes:
            take(self.wmap.get(s))
            for t2 in self.rmap.get(s, ()):
                take(t2)
        cred = self.credit.setdefault(body, {})
        waits = []
        for sem, val in sorted(need.items()):
            if cred.get(sem, 0) < val:
                waits.append((sem, val))
                tok = need_tok[sem]
                for s2, v2 in tok[2].items():
                    if cred.get(s2, 0) < v2:
                        cred[s2] = v2
                if cred.get(sem, 0) < val:
                    cred[sem] = val
        return waits, cred

    def _finish(self, body, sem, amt, fn, waits, cred, reads, writes, note):
        self.sem_names.add(sem)
        self.cnt[sem] = self.cnt.get(sem, 0) + amt
        newval = self.cnt[sem]
        clock = dict(cred)
        clock[sem] = newval
        tok = (sem, newval, clock)
        for s in writes:
            self.wmap[s] = tok
            self.rmap[s] = []
        for s in reads:
            self.rmap.setdefault(s, []).append(tok)
        self.ops.append((body, fn, waits, sem, amt, note))
        return tok

    def op(self, body, fn, reads=(), writes=(), note=""):
        assert body in ENG_SEMS
        if body == "g":
            reads = tuple(reads) + ("_g",)
            writes = tuple(writes) + ("_g",)
        waits, cred = self._deps(body, reads, writes)
        return self._finish(body, body, 1, fn, waits, cred, reads, writes, note)

    def dma(self, body, sem, fn, reads=(), writes=(), note=""):
        assert body in ("gd", "sd")
        waits, cred = self._deps(body, reads, writes)
        return self._finish(body, sem, 16, fn, waits, cred, reads, writes, note)

    def dma_batch(self, body, sem, fns, reads=(), writes=(), note=""):
        waits, cred = self._deps(body, reads, writes)
        tok = None
        for i, fn in enumerate(fns):
            w = waits if i == 0 else []
            tok = self._finish(body, sem, 16, fn, w, cred, (), (), f"{note}#{i}")
        for s in writes:
            self.wmap[s] = tok
            self.rmap[s] = []
        for s in reads:
            self.rmap.setdefault(s, []).append(tok)
        return tok

    def fence(self, src_slots, dst_slots):
        toks = []
        for s in src_slots:
            t = self.wmap.get(s)
            if t is not None:
                toks.append(t)
            toks.extend(self.rmap.get(s, ()))
        for d in dst_slots:
            self.rmap.setdefault(d, []).extend(toks)

    def replay(self, semh):
        def make_body(tags):
            def body(eng):
                for (btag, fn, waits, sem, amt, note) in self.ops:
                    if btag not in tags:
                        continue
                    for (s, val) in waits:
                        eng.wait_ge(semh[s], val)
                    last = fn(eng)
                    assert last is not None, f"op {note} returned no instruction"
                    last.then_inc(semh[sem], amt)
            return body
        return make_body


def build_program():
    nc = bass.Bass("TRN2", target_bir_lowering=False)

    xtm_d = nc.dram_tensor("xtm", [T, DM], F32, kind="ExternalInput").ap()
    xfm_d = nc.dram_tensor("xfm", [DM, T], F32, kind="ExternalInput").ap()
    wq_d = nc.dram_tensor("wq", [NPAIR, 128, DC * 128], F32, kind="ExternalInput").ap()
    wkv_d = nc.dram_tensor("wkv", [DM, 2 * HD], F32, kind="ExternalInput").ap()
    wp_d = nc.dram_tensor("wp", [DM, DM], F32, kind="ExternalInput").ap()
    brep_d = nc.dram_tensor("brep", [128, DM], F32, kind="ExternalInput").ap()
    wg_d = nc.dram_tensor("wg", [HC, 128, DC * 128], I8, kind="ExternalInput").ap()
    wu_d = nc.dram_tensor("wu", [HC, 128, DC * 128], I8, kind="ExternalInput").ap()
    wd_d = nc.dram_tensor("wd", [HID, DM], I8, kind="ExternalInput").ap()
    x2_d = nc.dram_tensor("x2scr", [T, DM], BF16, kind="Internal").ap()
    y_d = nc.dram_tensor("y", [T, DM], F32, kind="ExternalOutput").ap()

    P = Prog()

    with ExitStack() as ctx:
        sb = lambda name, shape, dt: ctx.enter_context(nc.sbuf_tensor(name, shape, dt))
        wq_sb = sb("wqsb", [128, 2, DC, 128], BF16)
        wp_sb = sb("wpsb", [128, DC, DM], BF16)
        wkv_sb = sb("wkvsb", [128, DC, 2 * HD], BF16)
        wd_sb = sb("wdres", [128, HC, DM], BF16)
        xbf_sb = sb("xbf", [128, DC, ST], BF16)
        H_sb = sb("hbuf", [128, HC, ST], BF16)
        pool = sb("pool", [128, 16384], BF16)
        x2s_sb = sb("x2s", [128, 3, DM], BF16)
        x0_sb = sb("x0", [128, 2, DM], F32)
        r1_sb = sb("r1", [128, 2, DM], F32)
        silu_sb = sb("silu", [128, 2, DM], F32)
        wgw_sb = sb("wgw", [128, 2, DC, 128], BF16)
        wuw_sb = sb("wuw", [128, 2, DC, 128], BF16)
        identB = sb("identB", [128, 128], BF16)
        identB2 = sb("identB2", [128, 64], BF16)
        ones_sb = sb("ones1", [1, 128], BF16)
        bbf_sb = sb("bbf", [1, DM], BF16)
        cpool = sb("cpool", [128, 8], F32)
        lnst_sb = sb("lnst", [128, 2, 8], F32)
        sums_sb = sb("sums", [128, 2, 4], F32)
        rr_sb = sb("rr", [128, 2, 4], F32)
        psall = ctx.enter_context(nc.psum_tensor("ps", [128, 8, 512], F32))

        pap = pool[:, :]
        O_v = pap[:, 0:8192].rearrange("p (k w) -> p k w", k=DC)
        Q_v = pap[:, 8192:10240].rearrange("p (j w) -> p j w", j=2)
        E_v = pap[:, 10240:12288].rearrange("p (a h t u) -> p a h t u", a=2, h=2, t=2)
        A_v = pap[:, 12288:14336].rearrange("p (a h t u) -> p a h t u", a=2, h=2, t=2)
        AT_v = pap[:, 14336:16384].rearrange("p (a h c w) -> p a h c w", a=2, h=2, c=2)
        r1b = r1_sb[:, :, :].rearrange("p a w -> p (a w)").bitcast(BF16)
        vtmp_v = r1b[:, 0:4096]
        krep_v = r1b[:, 1024:2048]
        vt_v = r1b[:, 2048:2560].rearrange("p (c d) -> p c d", c=8)

        def bank(i):
            return psall[:, i, :]

        def bank2(i):
            return psall[:, i:i + 2, :]

        def bankB(i):
            return psall[:, i, :].bitcast(BF16)

        CONSTS = [-CSHIFT, -0.5, FS, RN, EPS, 0.0, 1.0]
        for i, cv in enumerate(CONSTS):
            nc.const_aps.register(F32, float(cv), cpool[:, i:i + 1])

        def setup_fn(g):
            for i, cv in enumerate(CONSTS):
                g.memset(cpool[:, i:i + 1], float(cv))
            g.memset(ones_sb[:, :], 1.0)
            g.memset(identB[:, :], 0.0)
            g.memset(identB2[:, :], 0.0)
            g.drain()
            g.affine_select(out=identB[:, :], in_=identB[:, :],
                            compare_op=ALU.not_equal, fill=1.0, base=0,
                            pattern=[[-1, 128]], channel_multiplier=1)
            g.affine_select(out=identB2[:, :], in_=identB2[:, :],
                            compare_op=ALU.not_equal, fill=1.0, base=-64,
                            pattern=[[-1, 64]], channel_multiplier=1)
            g.drain()
            return g.nop()
        P.op("g", setup_fn, writes=("consts", "ident"), note="setup")

        def warm_fn(t):
            last = None
            for _ in range(80):
                last = t.matmul(bank(0)[:, 0:128], identB[:, :],
                                identB[:, :], start=True, stop=True)
            return last
        P.op("t", warm_fn, reads=("ident",), writes=("pb0",), note="warmup")

        P.dma("gd", "wkv", lambda g: g.dma_start(
            wkv_sb[:, :, :], wkv_d.rearrange("(k p) f -> p k f", p=128)),
            writes=("wkv",), note="ld wkv")

        def emit_wq(st, j):
            P.dma("gd", f"wq{j % 2}", lambda g, j=j: g.dma_start(
                wq_sb[:, j % 2, :, :],
                wq_d[j, :, :].rearrange("p (k f) -> p k f", k=DC)),
                writes=(f"wqt{j % 2}",), note=f"ld wq j{j} st{st}")

        for kk in range(DC):
            P.dma("gd", f"xc{kk}", lambda g, kk=kk: g.dma_start(
                xbf_sb[:, kk, :], xfm_d[kk * 128:(kk + 1) * 128, 0:ST]),
                reads=(), writes=(f"xc{kk}",), note=f"ld xbf st0 k{kk}")
            if kk == 0:
                emit_wq(0, 0)
                emit_wq(0, 1)
        XC = tuple(f"xc{k}" for k in range(DC))
        P.dma("gd", "brp", lambda g: g.dma_start(bbf_sb[:, :], brep_d[0:1, :]),
              writes=("brep",), note="ld b row")
        P.dma("gd", "wps", lambda g: g.dma_start(
            wp_sb[:, :, :], wp_d.rearrange("(k p) f -> p k f", p=128)),
            writes=("wp",), note="ld wp")
        P.dma("gd", "wdr", lambda g: g.dma_start(
            wd_sb[:, :, :], wd_d.rearrange("(k p) f -> p k f", p=128)),
            writes=("wdres",), note="ld wd")

        def mm_group(pairs, note, reads, writes):
            def fn(t, pairs=pairs):
                last = None
                for (o, l, r, st_, sp, tpos) in pairs:
                    last = t.matmul(o, l, r, start=st_, stop=sp,
                                    tile_position=tpos)
                return last
            return P.op("t", fn, reads=reads, writes=writes, note=note)

        def xslots(st):
            return XC

        def emit_attn_prefix(st):
            P.fence(("r10", "r11"), ("vtmp", "krep", "vt"))
            for n in range(2):
                for k in range(DC):
                    mm_group([(bank(5 + n)[:, 0:512], wkv_sb[:, k, :],
                               xbf_sb[:, k, n * 512:(n + 1) * 512],
                               k == 0, k == DC - 1, None)],
                             f"KV st{st} n{n} k{k}",
                             reads=(f"xc{k}", "wkv"), writes=(f"pb{5 + n}",))
            for n in range(2):
                P.op("v", lambda v, n=n: v.tensor_copy(
                    krep_v[0:64, n * 512:(n + 1) * 512], bank(5 + n)[0:64, :]),
                    reads=(f"pb{5 + n}",), writes=("krep",), note=f"Kevac{n}")
                P.op("v", lambda v, n=n: v.tensor_copy(
                    vtmp_v[64:128, n * 512:(n + 1) * 512], bank(5 + n)[64:128, :]),
                    reads=(f"pb{5 + n}",), writes=("vtmp",), note=f"Vevac{n}")
            P.dma("sd", "krp", lambda s: s.dma_start(krep_v[64:128, :], krep_v[0:64, :]),
                  reads=("krep",), writes=("krep",), note="krep dup")
            for c in range(8):
                def vt_fn(t, c=c):
                    return t.transpose(
                        bankB(7)[:, c * 64:(c + 1) * 64],
                        vtmp_v[64:128, c * 128:(c + 1) * 128],
                        identB2[64:128, 0:64])
                P.op("t", vt_fn, reads=("vtmp", "ident"), writes=("pb7",),
                     note=f"VT {c}")
            P.op("v", lambda v: v.tensor_copy(
                vt_v[:, :, :],
                bankB(7)[:, 0:512].rearrange("p (c d) -> p c d", c=8)),
                reads=("pb7",), writes=("vt",), note="VT evac")
            emit_q(st, 0, bm=BM_SOLO if st == 0 else BM_OVLP)

        def emit_q_half(st, j, n, per_chunk=False, bm=None):
            qb = (bm or BM_SOLO)["Q"][n]
            if per_chunk:
                for k in range(DC):
                    mm_group([(bank(qb)[:, 0:512], wq_sb[:, j % 2, k, :],
                               xbf_sb[:, k, n * 512:(n + 1) * 512],
                               k == 0, k == DC - 1, None)],
                             f"Q{j} n{n} k{k}",
                             reads=(f"xc{k}", f"wqt{j % 2}"), writes=(f"pb{qb}",))
            else:
                pairs = [(bank(qb)[:, 0:512], wq_sb[:, j % 2, k, :],
                          xbf_sb[:, k, n * 512:(n + 1) * 512],
                          k == 0, k == DC - 1, None)
                         for k in range(DC)]
                mm_group(pairs, f"Q{j} n{n} st{st}",
                         reads=xslots(st) + (f"wqt{j % 2}",), writes=(f"pb{qb}",))
            P.op("v", lambda v, n=n, j=j, qb=qb: v.tensor_scalar_mul(
                Q_v[:, j % 2, n * 512:(n + 1) * 512], bank(qb)[:, 0:512], QSCALE),
                reads=(f"pb{qb}",), writes=(f"Qp{j % 2}",), note=f"Qevac{j} n{n}")

        def emit_q(st, j, bm=None):
            pc = (st == 0 and j == 0)
            emit_q_half(st, j, 0, per_chunk=pc, bm=bm)
            emit_q_half(st, j, 1, per_chunk=pc, bm=bm)

        BM_SOLO = {"S": ((0, 1), (2, 3)), "Q": (5, 6)}
        BM_OVLP = {"S": ((5, 6), (5, 6)), "Q": (7, 7)}

        def emit_S(st, j, b, bm):
            par = b % 2
            for h in range(2):
                bk = bm["S"][par][h]
                pairs = []
                for wt in range(2):
                    lhsT = Q_v[h * 64:(h + 1) * 64, j % 2,
                               b * 256 + wt * 128: b * 256 + (wt + 1) * 128]
                    rhs = krep_v[h * 64:(h + 1) * 64, b * 256:(b + 1) * 256]
                    pairs.append((bank(bk)[:, wt * 256:(wt + 1) * 256],
                                  lhsT, rhs, True, True, None))
                mm_group(pairs, f"S j{j} b{b} h{h}",
                         reads=(f"Qp{j % 2}", "krep"), writes=(f"pb{bk}",))
            for h in range(2):
                bk = bm["S"][par][h]
                for wt in range(2):
                    def exp_fn(s, bk=bk, wt=wt, h=h, par=par):
                        return s.activation(
                            E_v[:, par, h, wt, :],
                            bank(bk)[:, wt * 256:(wt + 1) * 256],
                            AF.Exp, bias=-CSHIFT,
                            accum_out=sums_sb[:, par, 2 * h + wt: 2 * h + wt + 1])
                    P.op("c", exp_fn, reads=(f"pb{bk}", "consts"),
                         writes=(f"E{par}", f"sums{par}"),
                         note=f"exp j{j} b{b} h{h} wt{wt}")

        def emit_norm(j, b):
            par = b % 2
            P.op("v", lambda v, par=par: v.reciprocal(
                rr_sb[:, par, :], sums_sb[:, par, :]),
                reads=(f"sums{par}",), writes=(f"rr{par}",),
                note=f"recip j{j} b{b}")
            def norm_fn(v, par=par):
                rb = rr_sb[:, par, :].unsqueeze(2).broadcast_to([128, 4, 256])
                return v.scalar_tensor_tensor(
                    A_v[:, par, :, :, :].rearrange("p a b u -> p (a b) u"),
                    E_v[:, par, :, :, :].rearrange("p a b u -> p (a b) u"),
                    1.0, rb, ALU.mult, ALU.mult)
            P.op("v", norm_fn, reads=(f"E{par}", f"rr{par}"), writes=(f"A{par}",),
                 note=f"norm j{j} b{b}")

        def emit_AT(j, b, ev="v"):
            par = b % 2
            def at_fn(t, par=par):
                last = None
                for h in range(2):
                    for c in range(2):
                        for wt in range(2):
                            off = h * 512 + c * 256 + wt * 128
                            last = t.transpose(
                                bankB(7)[:, off:off + 128],
                                A_v[:, par, h, wt, c * 128:(c + 1) * 128],
                                identB[:, :])
                return last
            P.op("t", at_fn, reads=(f"A{par}", "ident"), writes=("pb7",),
                 note=f"AT j{j} b{b}")
            src = bankB(7)[:, 0:1024].rearrange("p (h c w) -> p h c w", h=2, c=2)
            if ev == "c":
                def atev_fn(s_, par=par, src=src):
                    return s_.activation(AT_v[:, par, :, :, :], src, AF.Copy)
                P.op("c", atev_fn, reads=("pb7", "consts"), writes=(f"AT{par}",),
                     note=f"ATevac j{j} b{b}")
            else:
                def atev_fn(v, par=par, src=src):
                    return v.tensor_copy(AT_v[:, par, :, :, :], src)
                P.op("v", atev_fn, reads=("pb7",), writes=(f"AT{par}",),
                     note=f"ATevac j{j} b{b}")

        def emit_AV(j, b, ev="v"):
            par = b % 2
            pairs = []
            for h in range(2):
                for c in range(2):
                    out = bank(4)[h * 64:(h + 1) * 64, par * 256:(par + 1) * 256]
                    pairs.append((out, vt_v[:, 2 * b + c, :],
                                  AT_v[:, par, h, c, :], c == 0, c == 1,
                                  (0, h * 64)))
            mm_group(pairs, f"AV j{j} b{b}", reads=(f"AT{par}", "vt"),
                     writes=("pb4",))
            if ev == "c":
                def oev_fn(s_, par=par, j=j, b=b):
                    return s_.activation(
                        O_v[:, j, b * 256:(b + 1) * 256],
                        bank(4)[:, par * 256:(par + 1) * 256], AF.Copy)
                P.op("c", oev_fn, reads=("pb4", "consts"), writes=(f"O{j}b{b}",),
                     note=f"Oevac j{j} b{b}")
            else:
                def oev_fn(v, par=par, j=j, b=b):
                    return v.tensor_copy(
                        O_v[:, j, b * 256:(b + 1) * 256],
                        bank(4)[:, par * 256:(par + 1) * 256])
                P.op("v", oev_fn, reads=("pb4",), writes=(f"O{j}b{b}",),
                     note=f"Oevac j{j} b{b}")

        NCH = NPAIR * NBLK

        def chain_stages(st, bm):
            out = []
            for i in range(NCH + 3):
                def stage(i=i, st=st, bm=bm):
                    if i < NCH:
                        j, b = divmod(i, NBLK)
                        emit_S(st, j, b, bm)
                        if j + 1 < NPAIR and b == 0:
                            emit_wq(st, j + 1)
                    if 1 <= i <= NCH:
                        j, b = divmod(i - 1, NBLK)
                        emit_norm(j, b)
                    if 2 <= i <= NCH + 1:
                        j, b = divmod(i - 2, NBLK)
                        emit_AT(j, b)
                    if 3 <= i <= NCH + 2:
                        j, b = divmod(i - 3, NBLK)
                        emit_AV(j, b)
                out.append(stage)
                if i < NCH:
                    j, b = divmod(i, NBLK)
                    if j + 1 < NPAIR and b in (2, 3):
                        out.append(lambda st=st, j=j, b=b, bm=bm:
                                   emit_q_half(st, j + 1, 0 if b == 2 else 1,
                                               bm=bm))
            return out

        def emit_x2t(st, wt, par=None):
            if par is None:
                par = wt % 3
            def x2t_fn(t, par=par):
                last = None
                for k in range(DC):
                    last = t.transpose(
                        bankB(4)[:, k * 128:(k + 1) * 128],
                        x2s_sb[:, par, k * 128:(k + 1) * 128],
                        identB[:, :])
                return last
            P.op("t", x2t_fn, reads=(f"x2s{par}", "ident"), writes=("pb4",),
                 note=f"x2T wt{wt}")
            def x2ev_fn(v, wt=wt):
                return v.tensor_copy(
                    xbf_sb[:, :, wt * 128:(wt + 1) * 128],
                    bankB(4)[:, 0:1024].rearrange("p (k w) -> p k w", k=DC))
            P.op("v", x2ev_fn, reads=("pb4",), writes=XC,
                 note=f"x2Tevac wt{wt}")

        def emit_proj_mm(st, wt):
            pb = 0 if wt % 2 == 0 else 2
            row0 = st * ST + wt * 128
            p2w = wt % 2
            P.dma("sd", f"x0{p2w}", lambda s, row0=row0, p2w=p2w: s.dma_start(
                x0_sb[:, p2w, :], xtm_d[row0:row0 + 128, :]),
                reads=(), writes=(f"x0_{p2w}",), note=f"x0 wt{wt}")
            pairs = []
            for k in range(DC):
                for n in range(2):
                    pairs.append((bank(pb + n)[:, 0:512],
                                  O_v[:, k, wt * 128:(wt + 1) * 128],
                                  wp_sb[:, k, n * 512:(n + 1) * 512],
                                  k == 0, False, None))
            for n in range(2):
                pairs.append((bank(pb + n)[:, 0:512], ones_sb[0:1, :],
                              bbf_sb[0:1, n * 512:(n + 1) * 512],
                              False, True, None))
            mm_group(pairs, f"proj wt{wt} st{st}",
                     reads=tuple(f"O{jj}b{wt // 2}" for jj in range(NPAIR))
                     + ("wp", "brep", "ident"),
                     writes=(f"pb{pb}", f"pb{pb + 1}"))

        def emit_proj_lnspill(st, wt, par, scratch="r1", defer=False):
            pb = 0 if wt % 2 == 0 else 2
            row0 = st * ST + wt * 128
            p2 = wt % 2
            rsb = r1_sb if scratch == "r1" else silu_sb
            out_op = emit_ln(
                P, nc, bank2(pb), x0_sb[:, p2, :], p2,
                x2s_sb[:, par, :], (f"pb{pb}", f"pb{pb + 1}"), (f"x0_{p2}",),
                f"x2s{par}", rsb, lnst_sb,
                rsb[:, 1 - p2, :], (f"{scratch}{1 - p2}",),
                f"LN1 st{st} wt{wt}", r_slot_prefix=scratch, defer_out=defer)

            def fin():
                if out_op is not None:
                    out_op()
                P.dma("sd", f"xs{par}", lambda s, row0=row0, par=par: s.dma_start(
                    x2_d[row0:row0 + 128, :], x2s_sb[:, par, :]),
                    reads=(f"x2s{par}",), writes=(f"x2d_{wt}",),
                    note=f"x2 spill wt{wt}")
            if defer:
                return fin
            fin()

        def emit_proj_ln(st, wt, par=None):
            if par is None:
                par = wt % 3
            emit_proj_mm(st, wt)
            emit_proj_lnspill(st, wt, par)

        def emit_projseg(st, drain=()):
            emit_proj_mm(st, 0)
            fin0 = emit_proj_lnspill(st, 0, 0, scratch="silu", defer=True)
            if drain:
                drain[0]()
            fin0()
            emit_proj_mm(st, 1)
            fin1 = emit_proj_lnspill(st, 1, 1, scratch="silu", defer=True)
            if len(drain) > 1:
                drain[1]()
            fin1()
            for f in drain[2:]:
                f()
            P.fence(("vtmp", "krep", "vt"), ("r10", "r11"))
            for m in range(2):
                emit_ffn_w(st, m, "wg", wgw_sb, wg_d)
                emit_ffn_w(st, m, "wu", wuw_sb, wu_d)
            for wt in range(2, 4):
                emit_proj_ln(st, wt)
                emit_x2t(st, wt - 2)
            emit_proj_ln(st, 4)
            emit_x2t(st, 2)
            emit_proj_ln(st, 5)
            emit_x2t(st, 3)
            gu_mm(st, 0, 0)
            emit_proj_ln(st, 6)
            emit_x2t(st, 4)
            gu_mm(st, 1, 0, which="g")
            emit_proj_ln(st, 7)
            emit_x2t(st, 5)
            emit_x2t(st, 6)
            emit_x2t(st, 7)
            gu_ev(st, 0, 0)
            gu_mm(st, 1, 0, which="u")
            gu_ev(st, 1, 0)
            gu_unit(st, 0, 1, pf=True)
            gu_unit(st, 1, 1, pf=True)
            for m in range(2, HC):
                gu_unit(st, m, 0, pf=False)
                gu_unit(st, m, 1, pf=True)

        def emit_ffn_w(st, m, which, dst_sb, src_d):
            def fn(g, m=m, dst_sb=dst_sb, src_d=src_d):
                return g.dma_start(
                    dst_sb[:, m % 2, :, :],
                    src_d[m, :, :].rearrange("p (k f) -> p k f", k=DC))
            P.dma("gd", f"{which}{m % 2}", fn, reads=(),
                  writes=(f"{which}{m % 2}",), note=f"ld {which} m{m} st{st}")

        def gu_mm(st, m, n, which="gu"):
            par = m % 2
            gb = 5 if par == 0 else 7
            if "g" in which:
                g_pairs = [(bank(gb)[:, 0:512], wgw_sb[:, par, k, :],
                            xbf_sb[:, k, n * 512:(n + 1) * 512],
                            k == 0, k == DC - 1, None) for k in range(DC)]
                mm_group(g_pairs, f"G m{m} n{n}", reads=XC + (f"wg{par}",),
                         writes=(f"pb{gb}",))
            if "u" in which:
                u_pairs = [(bank(6)[:, 0:512], wuw_sb[:, par, k, :],
                            xbf_sb[:, k, n * 512:(n + 1) * 512],
                            k == 0, k == DC - 1, None) for k in range(DC)]
                mm_group(u_pairs, f"U m{m} n{n}", reads=XC + (f"wu{par}",),
                         writes=("pb6",))

        def gu_ev(st, m, n):
            par = m % 2
            gb = 5 if par == 0 else 7
            def silu_fn(s_, par=par, n=n, gb=gb):
                return s_.activation(silu_sb[:, par, n * 512:(n + 1) * 512],
                                     bank(gb)[:, 0:512], AF.Silu, scale=FS)
            P.op("c", silu_fn, reads=(f"pb{gb}", "consts"),
                 writes=(f"silu{par}",), note=f"silu m{m} n{n}")
            def h_fn(v, par=par, m=m, n=n):
                return v.scalar_tensor_tensor(
                    H_sb[:, m, n * 512:(n + 1) * 512], bank(6)[:, 0:512], FS,
                    silu_sb[:, par, n * 512:(n + 1) * 512],
                    ALU.mult, ALU.mult)
            P.op("v", h_fn, reads=("pb6", f"silu{par}"),
                 writes=(f"H{m}",), note=f"H m{m} n{n}")

        def gu_unit(st, m, n, pf=True):
            gu_mm(st, m, n)
            if pf and m + 2 < HC:
                emit_ffn_w(st, m + 2, "wg", wgw_sb, wg_d)
                emit_ffn_w(st, m + 2, "wu", wuw_sb, wu_d)
            gu_ev(st, m, n)

        def down_units(st, wt):
            par = wt % 2
            db = 0 if par == 0 else 2
            row0 = st * ST + wt * 128
            units = []
            NSUB = 6
            ksub = HC // NSUB
            for sgi in range(NSUB):
                def mmsub(sgi=sgi, db=db, wt=wt):
                    pairs = []
                    for k in range(sgi * ksub, (sgi + 1) * ksub):
                        for n in range(2):
                            pairs.append((bank(db + n)[:, 0:512],
                                          H_sb[:, k, wt * 128:(wt + 1) * 128],
                                          wd_sb[:, k, n * 512:(n + 1) * 512],
                                          k == 0, k == HC - 1, None))
                    mm_group(pairs, f"down wt{wt} s{sgi}",
                             reads=tuple(f"H{k}" for k in
                                         range(sgi * ksub, (sgi + 1) * ksub))
                             + ("wdres",),
                             writes=(f"pb{db}", f"pb{db + 1}"))
                units.append(mmsub)

            def epi(db=db, par=par, row0=row0, wt=wt):
                p3 = wt % 3
                P.dma("sd", f"xr{p3}", lambda s: s.dma_start(
                    x2s_sb[:, p3, :], x2_d[row0:row0 + 128, :]),
                    reads=(f"x2d_{wt}",), writes=(f"x2s{p3}",),
                    note=f"x2 rd wt{wt}")
                emit_ln(P, nc, bank2(db), x2s_sb[:, p3, :], par,
                        x0_sb[:, par, :], (f"pb{db}", f"pb{db + 1}"),
                        (f"x2s{p3}",), f"x0_{par}", silu_sb, lnst_sb,
                        silu_sb[:, 1 - par, :], (f"silu{1 - par}",),
                        f"LN2 st{st} wt{wt}", r_slot_prefix="silu",
                        in0_scale=FS)
                P.dma("sd", f"st{par}", lambda s: s.dma_start(
                    y_d[row0:row0 + 128, :], x0_sb[:, par, :]),
                    reads=(f"x0_{par}",), writes=(), note=f"store wt{wt}")
            units.append(epi)
            return units

        def emit_downseg(st):
            nxt = st + 1 < NST
            dunits = []
            for wt in range(NBLK * 2):
                dunits.extend(down_units(st, wt))
            stages = []
            if nxt:
                for kk in range(DC):
                    P.dma("gd", f"xc{kk}", lambda g, st=st, kk=kk: g.dma_start(
                        xbf_sb[:, kk, :],
                        xfm_d[kk * 128:(kk + 1) * 128,
                              (st + 1) * ST:(st + 2) * ST]),
                        reads=(), writes=(f"xc{kk}",),
                        note=f"ld xbf st{st + 1} k{kk}")
                emit_wq(st + 1, 0)
                emit_wq(st + 1, 1)
                stages.append(lambda: emit_attn_prefix(st + 1))
                stages.extend(chain_stages(st + 1, BM_OVLP))
            tail = stages[-3:]
            stages = stages[:-3]
            LAG = 5
            ndu, nst_ = len(dunits), len(stages)
            di = si = 0
            while di < ndu or si < nst_:
                if di < ndu:
                    dunits[di]()
                    di += 1
                if si < nst_ and ((di - LAG) * nst_ >= si * ndu or di >= ndu):
                    stages[si]()
                    si += 1
            return tail

        emit_attn_prefix(0)
        stages0 = chain_stages(0, BM_SOLO)
        for f in stages0[:-3]:
            f()
        drain = stages0[-3:]
        for st in range(NST):
            emit_projseg(st, drain)
            drain = emit_downseg(st)

        semh = {}
        for sname in sorted(P.sem_names):
            semh[sname] = ctx.enter_context(nc.semaphore(f"sem_{sname}"))
        with nc.Block() as block:
            mk = P.replay(semh)
            block.tensor(mk(("t",)))
            block.vector(mk(("v",)))
            block.scalar(mk(("c",)))
            block.gpsimd(mk(("g", "gd")))

            def sync_body(eng):
                mk(("sd",))(eng)
                eng.wait_ge(semh["st0"], P.cnt["st0"])
                eng.wait_ge(semh["st1"], P.cnt["st1"])
            block.sync(sync_body)

    nc.finalize()
    return nc


def emit_ln(P, nc, psum_pair_ap, in1_ap, par, out_ap, psum_slots, in1_slots,
            out_slot, r_sb, lnst_sb, sq_ap, sq_slots, note, r_slot_prefix="r1",
            in0_scale=1.0, defer_out=False):
    r_ap = r_sb[:, par, :]
    r_slot = f"{r_slot_prefix}{par}"
    ln_slot = f"ln{par}"
    st_ = lambda i: lnst_sb[:, par, i:i + 1]

    def stt_fn(v):
        return v.scalar_tensor_tensor(
            r_ap.rearrange("p (a w) -> p a w", a=2), psum_pair_ap, in0_scale,
            in1_ap.rearrange("p (a w) -> p a w", a=2),
            ALU.mult, ALU.add, accum_out=st_(0))
    P.op("v", stt_fn, reads=psum_slots + in1_slots,
         writes=(r_slot, ln_slot), note=f"{note} stt")
    P.op("v", lambda v: v.tensor_scalar_mul(st_(1), st_(0), -RN),
         reads=(ln_slot,), writes=(ln_slot,), note=f"{note} negmean")
    P.op("c", lambda s: s.activation(sq_ap, r_ap, AF.Square,
                                     bias=st_(1), accum_out=st_(2)),
         reads=(r_slot, ln_slot, "consts"), writes=sq_slots + (ln_slot,),
         note=f"{note} sq")
    P.op("c", lambda s: s.activation(st_(3), st_(2), AF.Ln, scale=RN, bias=EPS),
         reads=(ln_slot, "consts"), writes=(ln_slot,), note=f"{note} ln")
    P.op("c", lambda s: s.activation(st_(4), st_(3), AF.Exp, scale=-0.5),
         reads=(ln_slot, "consts"), writes=(ln_slot,), note=f"{note} rstd")

    def out_op():
        P.op("v", lambda v: v.tensor_scalar(out_ap, r_ap, st_(1), st_(4),
                                            ALU.add, ALU.mult),
             reads=(r_slot, ln_slot), writes=(out_slot,), note=f"{note} out")
    if defer_out:
        return out_op
    out_op()


_NC_CACHE = []


def kernel(L0_attn_x, L0_attn_q_w, L0_attn_kv_w, L0_attn_proj_w, L0_attn_proj_b,
           L0_ffn_w_gate, L0_ffn_w_up, L0_ffn_w_down):
    x = np.asarray(L0_attn_x, dtype=np.float32).reshape(16384, 1024)
    wkvT = np.ascontiguousarray(np.asarray(L0_attn_kv_w, dtype=np.float32).T)
    wpT = np.ascontiguousarray(np.asarray(L0_attn_proj_w, dtype=np.float32).T)
    brep = np.ascontiguousarray(
        np.broadcast_to(np.asarray(L0_attn_proj_b, dtype=np.float32), (128, 1024)))
    def tile_w(wT, mc, kc):
        return np.ascontiguousarray(
            wT.reshape(kc, 128, mc, 128).transpose(2, 1, 0, 3).reshape(
                mc, 128, kc * 128))
    wqT = tile_w(np.asarray(L0_attn_q_w, dtype=np.float32).T, 8, 8)
    wgT = tile_w(np.asarray(L0_ffn_w_gate, dtype=np.int32).astype(np.int8).T, 24, 8)
    wuT = tile_w(np.asarray(L0_ffn_w_up, dtype=np.int32).astype(np.int8).T, 24, 8)
    wdT = np.ascontiguousarray(
        np.asarray(L0_ffn_w_down, dtype=np.int32).astype(np.int8).T)

    if not _NC_CACHE:
        _NC_CACHE.append(build_program())
    nc = _NC_CACHE[0]

    in_maps = []
    for c in range(NCORES):
        xs = np.ascontiguousarray(x[c * T:(c + 1) * T, :])
        xfs = np.ascontiguousarray(xs.T)
        in_maps.append({
            "xtm": xs, "xfm": xfs, "wq": wqT, "wkv": wkvT, "wp": wpT,
            "brep": brep, "wg": wgT, "wu": wuT, "wd": wdT,
        })
    res = run_bass_kernel_spmd(nc, in_maps, core_ids=list(range(NCORES)))
    out = np.concatenate([res.results[c]["y"] for c in range(NCORES)], axis=0)
    return out.reshape(1, 16384, 1024).astype(np.float32)


# version 33
# speedup vs baseline: 1.0342x; 1.0342x over previous
import numpy as np
from contextlib import ExitStack

import concourse.bass as bass
import concourse.mybir as mybir
from concourse.bass_utils import run_bass_kernel_spmd

F32 = mybir.dt.float32
BF16 = mybir.dt.bfloat16
I32 = mybir.dt.int32
I8 = mybir.dt.int8
AF = mybir.ActivationFunctionType
ALU = mybir.AluOpType

NCORES = 8
DM = 1024
DC = 8
HID = 3072
HC = 24
NH = 16
HD = 64
WIN = 256
T = 2048
ST = 1024
NST = T // ST
NBLK = ST // WIN
NPAIR = 8
CSHIFT = 32.0
QSCALE = 0.125
FS = 1.0 / 127.0
EPS = 1e-5
RN = 1.0 / DM

ENG_SEMS = ("t", "v", "c", "g")


class Prog:

    def __init__(self):
        self.ops = []
        self.wmap = {}
        self.rmap = {}
        self.cnt = {}
        self.credit = {}
        self.sem_names = set(ENG_SEMS)

    def _deps(self, body, reads, writes):
        need, need_tok = {}, {}

        def take(tok):
            if tok is None:
                return
            sem, val, _clock = tok
            if need.get(sem, 0) < val:
                need[sem] = val
                need_tok[sem] = tok

        for s in reads:
            take(self.wmap.get(s))
        for s in writes:
            take(self.wmap.get(s))
            for t2 in self.rmap.get(s, ()):
                take(t2)
        cred = self.credit.setdefault(body, {})
        waits = []
        for sem, val in sorted(need.items()):
            if cred.get(sem, 0) < val:
                waits.append((sem, val))
                tok = need_tok[sem]
                for s2, v2 in tok[2].items():
                    if cred.get(s2, 0) < v2:
                        cred[s2] = v2
                if cred.get(sem, 0) < val:
                    cred[sem] = val
        return waits, cred

    def _finish(self, body, sem, amt, fn, waits, cred, reads, writes, note):
        self.sem_names.add(sem)
        self.cnt[sem] = self.cnt.get(sem, 0) + amt
        newval = self.cnt[sem]
        clock = dict(cred)
        clock[sem] = newval
        tok = (sem, newval, clock)
        for s in writes:
            self.wmap[s] = tok
            self.rmap[s] = []
        for s in reads:
            self.rmap.setdefault(s, []).append(tok)
        self.ops.append((body, fn, waits, sem, amt, note))
        return tok

    def op(self, body, fn, reads=(), writes=(), note=""):
        assert body in ENG_SEMS
        if body == "g":
            reads = tuple(reads) + ("_g",)
            writes = tuple(writes) + ("_g",)
        waits, cred = self._deps(body, reads, writes)
        return self._finish(body, body, 1, fn, waits, cred, reads, writes, note)

    def dma(self, body, sem, fn, reads=(), writes=(), note=""):
        assert body in ("gd", "sd")
        waits, cred = self._deps(body, reads, writes)
        return self._finish(body, sem, 16, fn, waits, cred, reads, writes, note)

    def dma_batch(self, body, sem, fns, reads=(), writes=(), note=""):
        waits, cred = self._deps(body, reads, writes)
        tok = None
        for i, fn in enumerate(fns):
            w = waits if i == 0 else []
            tok = self._finish(body, sem, 16, fn, w, cred, (), (), f"{note}#{i}")
        for s in writes:
            self.wmap[s] = tok
            self.rmap[s] = []
        for s in reads:
            self.rmap.setdefault(s, []).append(tok)
        return tok

    def fence(self, src_slots, dst_slots):
        toks = []
        for s in src_slots:
            t = self.wmap.get(s)
            if t is not None:
                toks.append(t)
            toks.extend(self.rmap.get(s, ()))
        for d in dst_slots:
            self.rmap.setdefault(d, []).extend(toks)

    def replay(self, semh):
        def make_body(tags):
            def body(eng):
                for (btag, fn, waits, sem, amt, note) in self.ops:
                    if btag not in tags:
                        continue
                    for (s, val) in waits:
                        eng.wait_ge(semh[s], val)
                    last = fn(eng)
                    assert last is not None, f"op {note} returned no instruction"
                    last.then_inc(semh[sem], amt)
            return body
        return make_body


def build_program():
    nc = bass.Bass("TRN2", target_bir_lowering=False)

    xtm_d = nc.dram_tensor("xtm", [T, DM], F32, kind="ExternalInput").ap()
    xfm_d = nc.dram_tensor("xfm", [DM, T], F32, kind="ExternalInput").ap()
    wq_d = nc.dram_tensor("wq", [NPAIR, 128, DC * 128], F32, kind="ExternalInput").ap()
    wkv_d = nc.dram_tensor("wkv", [DM, 2 * HD], F32, kind="ExternalInput").ap()
    wp_d = nc.dram_tensor("wp", [DM, DM], F32, kind="ExternalInput").ap()
    brep_d = nc.dram_tensor("brep", [128, DM], F32, kind="ExternalInput").ap()
    wg_d = nc.dram_tensor("wg", [HC, 128, DC * 128], I8, kind="ExternalInput").ap()
    wu_d = nc.dram_tensor("wu", [HC, 128, DC * 128], I8, kind="ExternalInput").ap()
    wd_d = nc.dram_tensor("wd", [HID, DM], I8, kind="ExternalInput").ap()
    x2_d = nc.dram_tensor("x2scr", [T, DM], BF16, kind="Internal").ap()
    y_d = nc.dram_tensor("y", [T, DM], F32, kind="ExternalOutput").ap()

    P = Prog()

    with ExitStack() as ctx:
        sb = lambda name, shape, dt: ctx.enter_context(nc.sbuf_tensor(name, shape, dt))
        wq_sb = sb("wqsb", [128, 2, DC, 128], BF16)
        wp_sb = sb("wpsb", [128, DC, DM], BF16)
        wkv_sb = sb("wkvsb", [128, DC, 2 * HD], BF16)
        wd_sb = sb("wdres", [128, HC, DM], BF16)
        xbf_sb = sb("xbf", [128, DC, ST], BF16)
        H_sb = sb("hbuf", [128, HC, ST], BF16)
        pool = sb("pool", [128, 16384], BF16)
        x2s_sb = sb("x2s", [128, 3, DM], BF16)
        x0_sb = sb("x0", [128, 2, DM], F32)
        r1_sb = sb("r1", [128, 2, DM], F32)
        silu_sb = sb("silu", [128, 2, DM], F32)
        wgw_sb = sb("wgw", [128, 2, DC, 128], BF16)
        wuw_sb = sb("wuw", [128, 2, DC, 128], BF16)
        identB = sb("identB", [128, 128], BF16)
        identB2 = sb("identB2", [128, 64], BF16)
        ones_sb = sb("ones1", [1, 128], BF16)
        bbf_sb = sb("bbf", [1, DM], BF16)
        cpool = sb("cpool", [128, 8], F32)
        lnst_sb = sb("lnst", [128, 2, 8], F32)
        psall = ctx.enter_context(nc.psum_tensor("ps", [128, 8, 512], F32))

        pap = pool[:, :]
        O_v = pap[:, 0:8192].rearrange("p (k w) -> p k w", k=DC)
        Q_v = pap[:, 8192:10240].rearrange("p (j w) -> p j w", j=2)
        E3_v = pap[:, 10240:13312].rearrange("p (e x) -> p e x", e=3)
        rr_v = pap[:, 13312:14336].bitcast(F32).rearrange(
            "p (a w) -> p a w", a=2)
        ones64_v = pap[:, 14336:14400]
        r1b = r1_sb[:, :, :].rearrange("p a w -> p (a w)").bitcast(BF16)
        vtmp_v = r1b[:, 0:4096]
        krep_v = r1b[:, 1024:2048]
        vt_v = r1b[:, 2048:2560].rearrange("p (c d) -> p c d", c=8)

        def bank(i):
            return psall[:, i, :]

        def bank2(i):
            return psall[:, i:i + 2, :]

        def bankB(i):
            return psall[:, i, :].bitcast(BF16)

        CONSTS = [-CSHIFT, -0.5, FS, RN, EPS, 0.0, 1.0]
        for i, cv in enumerate(CONSTS):
            nc.const_aps.register(F32, float(cv), cpool[:, i:i + 1])

        def setup_fn(g):
            for i, cv in enumerate(CONSTS):
                g.memset(cpool[:, i:i + 1], float(cv))
            g.memset(ones_sb[:, :], 1.0)
            g.memset(ones64_v[:, :], 1.0)
            g.memset(identB[:, :], 0.0)
            g.memset(identB2[:, :], 0.0)
            g.drain()
            g.affine_select(out=identB[:, :], in_=identB[:, :],
                            compare_op=ALU.not_equal, fill=1.0, base=0,
                            pattern=[[-1, 128]], channel_multiplier=1)
            g.affine_select(out=identB2[:, :], in_=identB2[:, :],
                            compare_op=ALU.not_equal, fill=1.0, base=-64,
                            pattern=[[-1, 64]], channel_multiplier=1)
            g.drain()
            return g.nop()
        P.op("g", setup_fn, writes=("consts", "ident"), note="setup")

        def warm_fn(t):
            last = None
            for _ in range(80):
                last = t.matmul(bank(0)[:, 0:128], identB[:, :],
                                identB[:, :], start=True, stop=True)
            return last
        P.op("t", warm_fn, reads=("ident",), writes=("pb0",), note="warmup")

        P.dma("gd", "wkv", lambda g: g.dma_start(
            wkv_sb[:, :, :], wkv_d.rearrange("(k p) f -> p k f", p=128)),
            writes=("wkv",), note="ld wkv")

        def emit_wq(st, j):
            P.dma("gd", f"wq{j % 2}", lambda g, j=j: g.dma_start(
                wq_sb[:, j % 2, :, :],
                wq_d[j, :, :].rearrange("p (k f) -> p k f", k=DC)),
                writes=(f"wqt{j % 2}",), note=f"ld wq j{j} st{st}")

        for kk in range(DC):
            P.dma("gd", f"xc{kk}", lambda g, kk=kk: g.dma_start(
                xbf_sb[:, kk, :], xfm_d[kk * 128:(kk + 1) * 128, 0:ST]),
                reads=(), writes=(f"xc{kk}",), note=f"ld xbf st0 k{kk}")
            if kk == 0:
                emit_wq(0, 0)
                emit_wq(0, 1)
        XC = tuple(f"xc{k}" for k in range(DC))
        P.dma("gd", "brp", lambda g: g.dma_start(bbf_sb[:, :], brep_d[0:1, :]),
              writes=("brep",), note="ld b row")
        P.dma("gd", "wps", lambda g: g.dma_start(
            wp_sb[:, :, :], wp_d.rearrange("(k p) f -> p k f", p=128)),
            writes=("wp",), note="ld wp")
        P.dma("gd", "wdr", lambda g: g.dma_start(
            wd_sb[:, :, :], wd_d.rearrange("(k p) f -> p k f", p=128)),
            writes=("wdres",), note="ld wd")

        def mm_group(pairs, note, reads, writes):
            def fn(t, pairs=pairs):
                last = None
                for (o, l, r, st_, sp, tpos) in pairs:
                    last = t.matmul(o, l, r, start=st_, stop=sp,
                                    tile_position=tpos)
                return last
            return P.op("t", fn, reads=reads, writes=writes, note=note)

        def xslots(st):
            return XC

        def emit_attn_prefix(st):
            P.fence(("r10", "r11"), ("vtmp", "krep", "vt"))
            for n in range(2):
                for k in range(DC):
                    mm_group([(bank(5 + n)[:, 0:512], wkv_sb[:, k, :],
                               xbf_sb[:, k, n * 512:(n + 1) * 512],
                               k == 0, k == DC - 1, None)],
                             f"KV st{st} n{n} k{k}",
                             reads=(f"xc{k}", "wkv"), writes=(f"pb{5 + n}",))
            for n in range(2):
                P.op("v", lambda v, n=n: v.tensor_copy(
                    krep_v[0:64, n * 512:(n + 1) * 512], bank(5 + n)[0:64, :]),
                    reads=(f"pb{5 + n}",), writes=("krep",), note=f"Kevac{n}")
                P.op("v", lambda v, n=n: v.tensor_copy(
                    vtmp_v[64:128, n * 512:(n + 1) * 512], bank(5 + n)[64:128, :]),
                    reads=(f"pb{5 + n}",), writes=("vtmp",), note=f"Vevac{n}")
            P.dma("sd", "krp", lambda s: s.dma_start(krep_v[64:128, :], krep_v[0:64, :]),
                  reads=("krep",), writes=("krep",), note="krep dup")
            for c in range(8):
                def vt_fn(t, c=c):
                    return t.transpose(
                        bankB(7)[:, c * 64:(c + 1) * 64],
                        vtmp_v[64:128, c * 128:(c + 1) * 128],
                        identB2[64:128, 0:64])
                P.op("t", vt_fn, reads=("vtmp", "ident"), writes=("pb7",),
                     note=f"VT {c}")
            P.op("v", lambda v: v.tensor_copy(
                vt_v[:, :, :],
                bankB(7)[:, 0:512].rearrange("p (c d) -> p c d", c=8)),
                reads=("pb7",), writes=("vt",), note="VT evac")
            emit_q(st, 0, bm=BM_SOLO if st == 0 else BM_OVLP)

        def emit_q_half(st, j, n, per_chunk=False, bm=None):
            qb = (bm or BM_SOLO)["Q"][n]
            if per_chunk:
                for k in range(DC):
                    mm_group([(bank(qb)[:, 0:512], wq_sb[:, j % 2, k, :],
                               xbf_sb[:, k, n * 512:(n + 1) * 512],
                               k == 0, k == DC - 1, None)],
                             f"Q{j} n{n} k{k}",
                             reads=(f"xc{k}", f"wqt{j % 2}"), writes=(f"pb{qb}",))
            else:
                pairs = [(bank(qb)[:, 0:512], wq_sb[:, j % 2, k, :],
                          xbf_sb[:, k, n * 512:(n + 1) * 512],
                          k == 0, k == DC - 1, None)
                         for k in range(DC)]
                mm_group(pairs, f"Q{j} n{n} st{st}",
                         reads=xslots(st) + (f"wqt{j % 2}",), writes=(f"pb{qb}",))
            P.op("v", lambda v, n=n, j=j, qb=qb: v.tensor_scalar_mul(
                Q_v[:, j % 2, n * 512:(n + 1) * 512], bank(qb)[:, 0:512], QSCALE),
                reads=(f"pb{qb}",), writes=(f"Qp{j % 2}",), note=f"Qevac{j} n{n}")

        def emit_q(st, j, bm=None):
            pc = (st == 0 and j == 0)
            emit_q_half(st, j, 0, per_chunk=pc, bm=bm)
            emit_q_half(st, j, 1, per_chunk=pc, bm=bm)

        BM_SOLO = {"S": ((0, 1), (2, 3)), "Q": (5, 6)}
        BM_OVLP = {"S": ((5, 6), (5, 6)), "Q": (7, 7)}

        def emit_S(st, j, b, bm):
            par = b % 2
            e3 = (j * NBLK + b) % 3
            for h in range(2):
                bk = bm["S"][par][h]
                pairs = []
                for uh in range(2):
                    lhsT = krep_v[h * 64:(h + 1) * 64,
                                  b * 256 + uh * 128: b * 256 + (uh + 1) * 128]
                    rhs = Q_v[h * 64:(h + 1) * 64, j % 2, b * 256:(b + 1) * 256]
                    pairs.append((bank(bk)[:, uh * 256:(uh + 1) * 256],
                                  lhsT, rhs, True, True, None))
                mm_group(pairs, f"S j{j} b{b} h{h}",
                         reads=(f"Qp{j % 2}", "krep"), writes=(f"pb{bk}",))
            for h in range(2):
                bk = bm["S"][par][h]
                def exp_fn(s, bk=bk, h=h, e3=e3):
                    return s.activation(
                        E3_v[:, e3, h * 512:(h + 1) * 512],
                        bank(bk)[:, 0:512], AF.Exp, bias=-CSHIFT)
                P.op("c", exp_fn, reads=(f"pb{bk}", "consts"),
                     writes=(f"E{e3}",), note=f"exp j{j} b{b} h{h}")

        def emit_rrep(j, b):
            par = b % 2
            e3 = (j * NBLK + b) % 3
            pairs = []
            for h in range(2):
                for c in range(2):
                    out = bank(7)[h * 64:(h + 1) * 64, par * 256:(par + 1) * 256]
                    pairs.append((out, ones64_v[:, :],
                                  E3_v[:, e3, h * 512 + c * 256:
                                       h * 512 + (c + 1) * 256],
                                  c == 0, c == 1, (0, h * 64)))
            mm_group(pairs, f"R j{j} b{b}", reads=(f"E{e3}", "ident"),
                     writes=("pb7",))
            P.op("v", lambda v, par=par: v.reciprocal(
                rr_v[:, par, :], bank(7)[:, par * 256:(par + 1) * 256]),
                reads=("pb7",), writes=(f"rr{par}",), note=f"recip j{j} b{b}")

        def emit_AV(j, b):
            par = b % 2
            e3 = (j * NBLK + b) % 3
            pairs = []
            for h in range(2):
                for c in range(2):
                    out = bank(4)[h * 64:(h + 1) * 64, par * 256:(par + 1) * 256]
                    pairs.append((out, vt_v[:, 2 * b + c, :],
                                  E3_v[:, e3, h * 512 + c * 256:
                                       h * 512 + (c + 1) * 256],
                                  c == 0, c == 1, (0, h * 64)))
            mm_group(pairs, f"AV j{j} b{b}", reads=(f"E{e3}", "vt"),
                     writes=("pb4",))
            def ostt_fn(v, par=par, j=j, b=b):
                return v.scalar_tensor_tensor(
                    O_v[:, j, b * 256:(b + 1) * 256],
                    bank(4)[:, par * 256:(par + 1) * 256], 1.0,
                    rr_v[:, par, :], ALU.mult, ALU.mult)
            P.op("v", ostt_fn, reads=("pb4", f"rr{par}"),
                 writes=(f"O{j}b{b}",), note=f"Onorm j{j} b{b}")

        NCH = NPAIR * NBLK

        def chain_stages(st, bm):
            out = []
            for i in range(NCH + 2):
                def stage(i=i, st=st, bm=bm):
                    if i < NCH:
                        j, b = divmod(i, NBLK)
                        emit_S(st, j, b, bm)
                        if j + 1 < NPAIR and b == 0:
                            emit_wq(st, j + 1)
                    if 1 <= i <= NCH:
                        j, b = divmod(i - 1, NBLK)
                        emit_rrep(j, b)
                    if 2 <= i <= NCH + 1:
                        j, b = divmod(i - 2, NBLK)
                        emit_AV(j, b)
                out.append(stage)
                if i < NCH:
                    j, b = divmod(i, NBLK)
                    if j + 1 < NPAIR and b in (2, 3):
                        out.append(lambda st=st, j=j, b=b, bm=bm:
                                   emit_q_half(st, j + 1, 0 if b == 2 else 1,
                                               bm=bm))
            return out

        def emit_x2t(st, wt, par=None):
            if par is None:
                par = wt % 3
            def x2t_fn(t, par=par):
                last = None
                for k in range(DC):
                    last = t.transpose(
                        bankB(4)[:, k * 128:(k + 1) * 128],
                        x2s_sb[:, par, k * 128:(k + 1) * 128],
                        identB[:, :])
                return last
            P.op("t", x2t_fn, reads=(f"x2s{par}", "ident"), writes=("pb4",),
                 note=f"x2T wt{wt}")
            def x2ev_fn(v, wt=wt):
                return v.tensor_copy(
                    xbf_sb[:, :, wt * 128:(wt + 1) * 128],
                    bankB(4)[:, 0:1024].rearrange("p (k w) -> p k w", k=DC))
            P.op("v", x2ev_fn, reads=("pb4",), writes=XC,
                 note=f"x2Tevac wt{wt}")

        def emit_proj_mm(st, wt):
            pb = 0 if wt % 2 == 0 else 2
            row0 = st * ST + wt * 128
            p2w = wt % 2
            P.dma("sd", f"x0{p2w}", lambda s, row0=row0, p2w=p2w: s.dma_start(
                x0_sb[:, p2w, :], xtm_d[row0:row0 + 128, :]),
                reads=(), writes=(f"x0_{p2w}",), note=f"x0 wt{wt}")
            pairs = []
            for k in range(DC):
                for n in range(2):
                    pairs.append((bank(pb + n)[:, 0:512],
                                  O_v[:, k, wt * 128:(wt + 1) * 128],
                                  wp_sb[:, k, n * 512:(n + 1) * 512],
                                  k == 0, False, None))
            for n in range(2):
                pairs.append((bank(pb + n)[:, 0:512], ones_sb[0:1, :],
                              bbf_sb[0:1, n * 512:(n + 1) * 512],
                              False, True, None))
            mm_group(pairs, f"proj wt{wt} st{st}",
                     reads=tuple(f"O{jj}b{wt // 2}" for jj in range(NPAIR))
                     + ("wp", "brep", "ident"),
                     writes=(f"pb{pb}", f"pb{pb + 1}"))

        def emit_proj_lnspill(st, wt, par, scratch="r1", defer=False):
            pb = 0 if wt % 2 == 0 else 2
            row0 = st * ST + wt * 128
            p2 = wt % 2
            rsb = r1_sb if scratch == "r1" else silu_sb
            out_op = emit_ln(
                P, nc, bank2(pb), x0_sb[:, p2, :], p2,
                x2s_sb[:, par, :], (f"pb{pb}", f"pb{pb + 1}"), (f"x0_{p2}",),
                f"x2s{par}", rsb, lnst_sb,
                rsb[:, 1 - p2, :], (f"{scratch}{1 - p2}",),
                f"LN1 st{st} wt{wt}", r_slot_prefix=scratch, defer_out=defer)

            def fin():
                if out_op is not None:
                    out_op()
                P.dma("sd", f"xs{par}", lambda s, row0=row0, par=par: s.dma_start(
                    x2_d[row0:row0 + 128, :], x2s_sb[:, par, :]),
                    reads=(f"x2s{par}",), writes=(f"x2d_{wt}",),
                    note=f"x2 spill wt{wt}")
            if defer:
                return fin
            fin()

        def emit_proj_ln(st, wt, par=None):
            if par is None:
                par = wt % 3
            emit_proj_mm(st, wt)
            emit_proj_lnspill(st, wt, par)

        def emit_projseg(st, drain=()):
            emit_proj_mm(st, 0)
            fin0 = emit_proj_lnspill(st, 0, 0, scratch="silu", defer=True)
            if drain:
                drain[0]()
            fin0()
            emit_proj_mm(st, 1)
            fin1 = emit_proj_lnspill(st, 1, 1, scratch="silu", defer=True)
            if len(drain) > 1:
                drain[1]()
            fin1()
            for f in drain[2:]:
                f()
            P.fence(("vtmp", "krep", "vt"), ("r10", "r11"))
            for m in range(2):
                emit_ffn_w(st, m, "wg", wgw_sb, wg_d)
                emit_ffn_w(st, m, "wu", wuw_sb, wu_d)
            for wt in range(2, 4):
                emit_proj_ln(st, wt)
                if wt >= 2:
                    emit_x2t(st, wt - 2)
            emit_proj_ln(st, 4)
            emit_x2t(st, 2)
            emit_proj_ln(st, 5)
            emit_x2t(st, 3)
            punits = []
            for wt in range(6, NBLK * 2):
                punits.append(lambda wt=wt: emit_proj_ln(st, wt))
                punits.append(lambda wt=wt: emit_x2t(st, wt - 2))
            gu0 = [lambda m=m: gu_unit(st, m, 0) for m in range(HC)]
            pi = gi = 0
            while pi < len(punits) or gi < len(gu0):
                if pi < len(punits) and (gi * len(punits) >= pi * len(gu0)
                                         or gi >= len(gu0)):
                    punits[pi]()
                    pi += 1
                if gi < len(gu0):
                    gu0[gi]()
                    gi += 1
            emit_x2t(st, NBLK * 2 - 2)
            emit_x2t(st, NBLK * 2 - 1)
            emit_ffn_w(st, 0, "wg", wgw_sb, wg_d)
            emit_ffn_w(st, 0, "wu", wuw_sb, wu_d)
            emit_ffn_w(st, 1, "wg", wgw_sb, wg_d)
            emit_ffn_w(st, 1, "wu", wuw_sb, wu_d)
            for m in range(HC):
                gu_unit(st, m, 1)

        def emit_ffn_w(st, m, which, dst_sb, src_d):
            def fn(g, m=m, dst_sb=dst_sb, src_d=src_d):
                return g.dma_start(
                    dst_sb[:, m % 2, :, :],
                    src_d[m, :, :].rearrange("p (k f) -> p k f", k=DC))
            P.dma("gd", f"{which}{m % 2}", fn, reads=(),
                  writes=(f"{which}{m % 2}",), note=f"ld {which} m{m} st{st}")

        def gu_unit(st, m, n):
            par = m % 2
            gb = 5 if par == 0 else 7
            g_pairs, u_pairs = [], []
            for k in range(DC):
                g_pairs.append((bank(gb)[:, 0:512], wgw_sb[:, par, k, :],
                                xbf_sb[:, k, n * 512:(n + 1) * 512],
                                k == 0, k == DC - 1, None))
                u_pairs.append((bank(6)[:, 0:512], wuw_sb[:, par, k, :],
                                xbf_sb[:, k, n * 512:(n + 1) * 512],
                                k == 0, k == DC - 1, None))
            mm_group(g_pairs, f"G m{m} n{n}", reads=XC + (f"wg{par}",),
                     writes=(f"pb{gb}",))
            mm_group(u_pairs, f"U m{m} n{n}", reads=XC + (f"wu{par}",),
                     writes=("pb6",))
            if m + 2 < HC:
                emit_ffn_w(st, m + 2, "wg", wgw_sb, wg_d)
                emit_ffn_w(st, m + 2, "wu", wuw_sb, wu_d)
            def silu_fn(s_, par=par, n=n, gb=gb):
                return s_.activation(silu_sb[:, par, n * 512:(n + 1) * 512],
                                     bank(gb)[:, 0:512], AF.Silu, scale=FS)
            P.op("c", silu_fn, reads=(f"pb{gb}", "consts"),
                 writes=(f"silu{par}",), note=f"silu m{m} n{n}")
            def h_fn(v, par=par, m=m, n=n):
                return v.scalar_tensor_tensor(
                    H_sb[:, m, n * 512:(n + 1) * 512], bank(6)[:, 0:512], FS,
                    silu_sb[:, par, n * 512:(n + 1) * 512],
                    ALU.mult, ALU.mult)
            P.op("v", h_fn, reads=("pb6", f"silu{par}"),
                 writes=(f"H{m}",), note=f"H m{m} n{n}")

        def down_units(st, wt):
            par = wt % 2
            db = 0 if par == 0 else 2
            row0 = st * ST + wt * 128
            units = []
            NSUB = 6
            ksub = HC // NSUB
            for sgi in range(NSUB):
                def mmsub(sgi=sgi, db=db, wt=wt):
                    pairs = []
                    for k in range(sgi * ksub, (sgi + 1) * ksub):
                        for n in range(2):
                            pairs.append((bank(db + n)[:, 0:512],
                                          H_sb[:, k, wt * 128:(wt + 1) * 128],
                                          wd_sb[:, k, n * 512:(n + 1) * 512],
                                          k == 0, k == HC - 1, None))
                    mm_group(pairs, f"down wt{wt} s{sgi}",
                             reads=tuple(f"H{k}" for k in
                                         range(sgi * ksub, (sgi + 1) * ksub))
                             + ("wdres",),
                             writes=(f"pb{db}", f"pb{db + 1}"))
                units.append(mmsub)

            def epi(db=db, par=par, row0=row0, wt=wt):
                p3 = wt % 3
                P.dma("sd", f"xr{p3}", lambda s: s.dma_start(
                    x2s_sb[:, p3, :], x2_d[row0:row0 + 128, :]),
                    reads=(f"x2d_{wt}",), writes=(f"x2s{p3}",),
                    note=f"x2 rd wt{wt}")
                emit_ln(P, nc, bank2(db), x2s_sb[:, p3, :], par,
                        x0_sb[:, par, :], (f"pb{db}", f"pb{db + 1}"),
                        (f"x2s{p3}",), f"x0_{par}", silu_sb, lnst_sb,
                        silu_sb[:, 1 - par, :], (f"silu{1 - par}",),
                        f"LN2 st{st} wt{wt}", r_slot_prefix="silu",
                        in0_scale=FS)
                P.dma("sd", f"st{par}", lambda s: s.dma_start(
                    y_d[row0:row0 + 128, :], x0_sb[:, par, :]),
                    reads=(f"x0_{par}",), writes=(), note=f"store wt{wt}")
            units.append(epi)
            return units

        def emit_downseg(st):
            nxt = st + 1 < NST
            dunits = []
            for wt in range(NBLK * 2):
                dunits.extend(down_units(st, wt))
            stages = []
            if nxt:
                for kk in range(DC):
                    P.dma("gd", f"xc{kk}", lambda g, st=st, kk=kk: g.dma_start(
                        xbf_sb[:, kk, :],
                        xfm_d[kk * 128:(kk + 1) * 128,
                              (st + 1) * ST:(st + 2) * ST]),
                        reads=(), writes=(f"xc{kk}",),
                        note=f"ld xbf st{st + 1} k{kk}")
                emit_wq(st + 1, 0)
                emit_wq(st + 1, 1)
                stages.append(lambda: emit_attn_prefix(st + 1))
                stages.extend(chain_stages(st + 1, BM_OVLP))
            tail = stages[-2:]
            stages = stages[:-2]
            LAG = 5
            ndu, nst_ = len(dunits), len(stages)
            di = si = 0
            while di < ndu or si < nst_:
                if di < ndu:
                    dunits[di]()
                    di += 1
                if si < nst_ and ((di - LAG) * nst_ >= si * ndu or di >= ndu):
                    stages[si]()
                    si += 1
            return tail

        emit_attn_prefix(0)
        stages0 = chain_stages(0, BM_SOLO)
        for f in stages0[:-2]:
            f()
        drain = stages0[-2:]
        for st in range(NST):
            emit_projseg(st, drain)
            drain = emit_downseg(st)

        semh = {}
        for sname in sorted(P.sem_names):
            semh[sname] = ctx.enter_context(nc.semaphore(f"sem_{sname}"))
        with nc.Block() as block:
            mk = P.replay(semh)
            block.tensor(mk(("t",)))
            block.vector(mk(("v",)))
            block.scalar(mk(("c",)))
            block.gpsimd(mk(("g", "gd")))

            def sync_body(eng):
                mk(("sd",))(eng)
                eng.wait_ge(semh["st0"], P.cnt["st0"])
                eng.wait_ge(semh["st1"], P.cnt["st1"])
            block.sync(sync_body)

    nc.finalize()
    return nc


def emit_ln(P, nc, psum_pair_ap, in1_ap, par, out_ap, psum_slots, in1_slots,
            out_slot, r_sb, lnst_sb, sq_ap, sq_slots, note, r_slot_prefix="r1",
            in0_scale=1.0, defer_out=False):
    r_ap = r_sb[:, par, :]
    r_slot = f"{r_slot_prefix}{par}"
    ln_slot = f"ln{par}"
    st_ = lambda i: lnst_sb[:, par, i:i + 1]

    def stt_fn(v):
        return v.scalar_tensor_tensor(
            r_ap.rearrange("p (a w) -> p a w", a=2), psum_pair_ap, in0_scale,
            in1_ap.rearrange("p (a w) -> p a w", a=2),
            ALU.mult, ALU.add, accum_out=st_(0))
    P.op("v", stt_fn, reads=psum_slots + in1_slots,
         writes=(r_slot, ln_slot), note=f"{note} stt")
    P.op("v", lambda v: v.tensor_scalar_mul(st_(1), st_(0), -RN),
         reads=(ln_slot,), writes=(ln_slot,), note=f"{note} negmean")
    P.op("c", lambda s: s.activation(sq_ap, r_ap, AF.Square,
                                     bias=st_(1), accum_out=st_(2)),
         reads=(r_slot, ln_slot, "consts"), writes=sq_slots + (ln_slot,),
         note=f"{note} sq")
    P.op("c", lambda s: s.activation(st_(3), st_(2), AF.Ln, scale=RN, bias=EPS),
         reads=(ln_slot, "consts"), writes=(ln_slot,), note=f"{note} ln")
    P.op("c", lambda s: s.activation(st_(4), st_(3), AF.Exp, scale=-0.5),
         reads=(ln_slot, "consts"), writes=(ln_slot,), note=f"{note} rstd")

    def out_op():
        P.op("v", lambda v: v.tensor_scalar(out_ap, r_ap, st_(1), st_(4),
                                            ALU.add, ALU.mult),
             reads=(r_slot, ln_slot), writes=(out_slot,), note=f"{note} out")
    if defer_out:
        return out_op
    out_op()


_NC_CACHE = []


def kernel(L0_attn_x, L0_attn_q_w, L0_attn_kv_w, L0_attn_proj_w, L0_attn_proj_b,
           L0_ffn_w_gate, L0_ffn_w_up, L0_ffn_w_down):
    x = np.asarray(L0_attn_x, dtype=np.float32).reshape(16384, 1024)
    wkvT = np.ascontiguousarray(np.asarray(L0_attn_kv_w, dtype=np.float32).T)
    wpT = np.ascontiguousarray(np.asarray(L0_attn_proj_w, dtype=np.float32).T)
    brep = np.ascontiguousarray(
        np.broadcast_to(np.asarray(L0_attn_proj_b, dtype=np.float32), (128, 1024)))
    def tile_w(wT, mc, kc):
        return np.ascontiguousarray(
            wT.reshape(kc, 128, mc, 128).transpose(2, 1, 0, 3).reshape(
                mc, 128, kc * 128))
    wqT = tile_w(np.asarray(L0_attn_q_w, dtype=np.float32).T, 8, 8)
    wgT = tile_w(np.asarray(L0_ffn_w_gate, dtype=np.int32).astype(np.int8).T, 24, 8)
    wuT = tile_w(np.asarray(L0_ffn_w_up, dtype=np.int32).astype(np.int8).T, 24, 8)
    wdT = np.ascontiguousarray(
        np.asarray(L0_ffn_w_down, dtype=np.int32).astype(np.int8).T)

    if not _NC_CACHE:
        _NC_CACHE.append(build_program())
    nc = _NC_CACHE[0]

    in_maps = []
    for c in range(NCORES):
        xs = np.ascontiguousarray(x[c * T:(c + 1) * T, :])
        xfs = np.ascontiguousarray(xs.T)
        in_maps.append({
            "xtm": xs, "xfm": xfs, "wq": wqT, "wkv": wkvT, "wp": wpT,
            "brep": brep, "wg": wgT, "wu": wuT, "wd": wdT,
        })
    res = run_bass_kernel_spmd(nc, in_maps, core_ids=list(range(NCORES)))
    out = np.concatenate([res.results[c]["y"] for c in range(NCORES)], axis=0)
    return out.reshape(1, 16384, 1024).astype(np.float32)
